# version 02
# baseline (speedup 1.0000x reference)
import numpy as np
import ml_dtypes
import concourse.bass as bass
import concourse.mybir as mybir
from concourse.bass_utils import run_bass_kernel_spmd
from contextlib import ExitStack

F32 = mybir.dt.float32
BF16 = mybir.dt.bfloat16
I32 = mybir.dt.int32
ALU = mybir.AluOpType
AF = mybir.ActivationFunctionType
AX = mybir.AxisListType

D = 1024
HD = 4096
NE = 8
HN = 16
T = 256
JT = 2 * T
KD = D // 128
KH = HD // 128
HALFPI = float(np.pi / 2)
TWOPI = float(2 * np.pi)
INV2PI = float(1.0 / (2 * np.pi))

ENGS = ("tensor", "vector", "scalar", "sync", "gpsimd")


def build_program(L, debug=False):
    NCH = L // T
    nc = bass.Bass("TRN2", target_bir_lowering=False)

    dx = nc.dram_tensor("x", [L, D], F32, kind="ExternalInput").ap()
    d_inp = nc.dram_tensor("inproj", [D, HD], F32, kind="ExternalInput").ap()
    d_outp = nc.dram_tensor("outproj", [HD, D], F32, kind="ExternalInput").ap()
    d_alog = nc.dram_tensor("alog", [1, HN], F32, kind="ExternalInput").ap()
    d_bproj = nc.dram_tensor("bproj", [HD, HN], F32, kind="ExternalInput").ap()
    d_cproj = nc.dram_tensor("cproj", [HD, HN], F32, kind="ExternalInput").ap()
    d_dtp = nc.dram_tensor("dtp", [128, KH], F32, kind="ExternalInput").ap()
    d_skip = nc.dram_tensor("skip", [HD, HD], F32, kind="ExternalInput").ap()
    d_phase = nc.dram_tensor("phase", [128, KH], F32, kind="ExternalInput").ap()
    d_router = nc.dram_tensor("routerT", [128, KD * NE], F32, kind="ExternalInput").ap()
    d_w1 = nc.dram_tensor("w1", [NE, D, HD], F32, kind="ExternalInput").ap()
    d_b1 = nc.dram_tensor("b1t", [128, NE * KH], F32, kind="ExternalInput").ap()
    d_w2 = nc.dram_tensor("w2", [NE, HD, D], F32, kind="ExternalInput").ap()
    d_b2 = nc.dram_tensor("b2", [NE, D], F32, kind="ExternalInput").ap()
    d_ident = nc.dram_tensor("ident", [128, 128], F32, kind="ExternalInput").ap()
    d_ones1 = nc.dram_tensor("ones1", [1, 128], F32, kind="ExternalInput").ap()
    d_onescol = nc.dram_tensor("onescol", [128, 1], F32, kind="ExternalInput").ap()
    d_oh16 = nc.dram_tensor("oh16", [16, 16 * 128], BF16, kind="ExternalInput").ap()
    d_oh8 = nc.dram_tensor("oh8", [8, 8 * 128], BF16, kind="ExternalInput").ap()
    d_pos = nc.dram_tensor("posr", [1, L], F32, kind="ExternalInput").ap()
    d_out = nc.dram_tensor("out", [L, D], F32, kind="ExternalOutput").ap()
    dbg = {}
    if debug:
        for nm, shape in [("dbg_hd", [128, KH * T]), ("dbg_hde", [128, KH * T]),
                          ("dbg_bbc", [128, HN * T]), ("dbg_cbc", [128, T * HN]),
                          ("dbg_y0", [128, KH * T]), ("dbg_xmid0", [128, 2048]),
                          ("dbg_ln", [128, KD * JT]), ("dbg_wrow", [8, JT]),
                          ("dbg_ymrg", [128, KH * T]), ("dbg_xatt", [128, 2048]),
                          ("dbg_carry", [128, KH * HN]), ("dbg_l2", [128, 2048])]:
            dt_ = BF16 if nm in ("dbg_hd", "dbg_hde", "dbg_bbc", "dbg_cbc",
                                 "dbg_y0", "dbg_ymrg", "dbg_wrow", "dbg_ln") else F32
            dbg[nm] = nc.dram_tensor(nm, shape, dt_, kind="ExternalOutput").ap()

    s_w1s = nc.dram_tensor("w1s", [NE, 16, 128, 2048], BF16, kind="Internal").ap()
    s_w2s = nc.dram_tensor("w2s", [NE, 16, 128, 2048], BF16, kind="Internal").ap()
    s_skps = nc.dram_tensor("skps", [KH, 2, 128, 2048], BF16, kind="Internal").ap()
    s_inps = nc.dram_tensor("inps", [16, 128, 2048], BF16, kind="Internal").ap()
    s_outps = nc.dram_tensor("outps", [16, 128, 2048], BF16, kind="Internal").ap()

    ctx = ExitStack()
    with ctx:
        sb = lambda name, shape, dt_: ctx.enter_context(nc.sbuf_tensor(name, shape, dt_))
        c_ident = sb("c_ident", [128, 128], F32)
        c_ones1 = sb("c_ones1", [1, 128], F32)
        c_onescol = sb("c_onescol", [128, 1], F32)
        c_oh16 = sb("c_oh16", [16, 16 * 128], BF16)
        p_dt = sb("p_dt", [128, KH], F32)

        p_phabs = sb("p_phabs", [128, KH], F32)
        p_decay = sb("p_decay", [128, KH * HN], F32)
        p_carry = sb("p_carry", [128, KH * HN], F32)
        p_b1 = sb("p_b1", [128, NE * KH], BF16)
        p_b2 = sb("p_b2", [8, D], BF16)
        p_router = sb("p_router", [128, KD * NE], BF16)
        p_bstat = sb("p_bstat", [128, KH * HN], BF16)
        p_cstat = sb("p_cstat", [128, KH * HN], BF16)
        r_stat = sb("r_stat", [1, 768], F32)
        r_bc = sb("r_bc", [16, T], BF16)
        r_cc = sb("r_cc", [16, T], BF16)
        r_wrow2 = [sb(f"r_wrow{i}", [8, JT], BF16) for i in range(2)]
        q_lg = sb("q_lg", [8, T], F32)
        q_tlg = sb("q_tlg", [128, 16], F32)
        q_exps = sb("q_exps", [128, 16], F32)
        q_m = sb("q_m", [128, 8], F32)
        q_t = sb("q_t", [128, 16], F32)
        q_wtm = sb("q_wtm", [128, 16], F32)
        r_erow = q_t[0:1, 0:HN]
        r_ebc = q_exps
        pool_s = [sb(f"pool_s{i}", [128, 2048], BF16) for i in range(2)]
        pool_g = [sb(f"pool_g{i}", [128, 2048], BF16) for i in range(2)]
        x_tm = sb("x_tm", [128, 2048], F32)
        p_ndt = x_tm[:, 0:KH]
        r_pos = None
        x_fin = sb("x_fin", [128, 2048], F32)
        x_mid = [sb(f"x_mid{i}", [128, 2048], F32) for i in range(2)]
        x_att = sb("x_att", [128, 2048], F32)
        scratch8 = sb("scratch8", [128, 2048], F32)
        s_hd = sb("s_hd", [128, KH * T], BF16)
        s_hde = sb("s_hde", [128, KH * T], BF16)
        s_bbc = sb("s_bbc", [128, HN * T], BF16)
        s_cbc = sb("s_cbc", [128, T * HN], BF16)
        s_y = sb("s_y", [128, KH * T], BF16)
        s_hsc = sb("s_hsc", [128, T * HN], BF16)
        s_v = [sb("s_v0", [128, T], BF16)] * 2
        s_ln2 = [sb(f"s_ln{i}", [128, KD * JT], BF16) for i in range(2)]
        s_h = sb("s_h", [128, KH * JT], BF16)
        s_wbc = sb("s_wbc", [128, 2 * JT], BF16)
        ps = [ctx.enter_context(nc.psum_tensor(f"ps{i}", [128, 512], F32)) for i in range(8)]

        sems = {}
        for nm in ("mt", "mv", "ms", "mld", "mc0", "mc1", "mx", "mp", "mst",
                   "ms0", "ms1", "ms2", "mg0", "mg1"):
            sems[nm] = [ctx.enter_context(nc.semaphore(nm)), 0]
        waited = {e: {} for e in ENGS}

        def mark(eng, instr, snm, step=1):
            sems[snm][1] += step
            instr.then_inc(sems[snm][0], step)
            return (snm, sems[snm][1], eng)

        def waitm(eng, *mss, force=False):
            for m in mss:
                if m is None:
                    continue
                snm, cnt, prod = m
                if prod == eng and not force:
                    continue
                if waited[eng].get(snm, 0) >= cnt:
                    continue
                getattr(nc, eng).wait_ge(sems[snm][0], cnt)
                waited[eng][snm] = cnt

        class PingPong:
            def __init__(self, views):
                self.views = views
                self.hist = [None] * len(views)
                self.i = 0
            def acquire(self, eng="tensor"):
                s = self.i % len(self.views)
                self.i += 1
                waitm(eng, self.hist[s])
                return self.views[s]
            def release(self, b, milestone):
                self.hist[self.views.index(b)] = milestone

        ppA = PingPong([0, 1, 6])
        ppB = PingPong([2, 3, 4, 5])
        ppC = ppB
        pp7 = PingPong([7])

        class Pool:
            def __init__(self, bufs, qname, slot_sems):
                self.bufs = bufs
                self.q = qname
                self.slot_sems = slot_sems
                self.hist = [None] * len(bufs)
                self.n = 0
            def fetch(self, src_ap, dst_map=None):
                s = self.n % len(self.bufs)
                self.n += 1
                waitm(self.q, self.hist[s])
                dst = self.bufs[s][:] if dst_map is None else dst_map(self.bufs[s])
                ins = getattr(nc, self.q).dma_start(out=dst, in_=src_ap)
                m = mark(self.q, ins, self.slot_sems[s], 16)
                return s, m
            def consumed(self, s, milestone):
                self.hist[s] = milestone

        pl_s = Pool(pool_s, "sync", ["ms0", "ms1"])
        pl_g = Pool(pool_g, "gpsimd", ["mg0", "mg1"])
        pools2 = [pl_g, pl_s]

        def fetch2(idx, src_ap):
            pl = pools2[idx % 2]
            s, m = pl.fetch(src_ap)
            return pl, s, m

        blk = ctx.enter_context(nc.Block())

        def gload(dst_ap, src_ap):
            ins = nc.gpsimd.dma_start(out=dst_ap, in_=src_ap)
            return mark("gpsimd", ins, "mld", 16)

        m_ci = gload(c_ident[:], d_ident)
        m_o1 = gload(c_ones1[:], d_ones1)
        m_oc = gload(c_onescol[:], d_onescol)
        m_oh16 = gload(c_oh16[:], d_oh16)
        m_dtp = gload(p_dt[:], d_dtp)
        m_ph = gload(p_phabs[:], d_phase)
        m_b1 = gload(p_b1[:], d_b1)
        m_b2 = gload(p_b2[:], d_b2)
        m_rt = gload(p_router[:], d_router)
        m_al = gload(r_erow, d_alog)
        m_bst = gload(p_bstat[:].rearrange("p (k n) -> p k n", n=HN),
                      d_bproj.rearrange("(k p) n -> p k n", p=128))
        m_cst = gload(p_cstat[:].rearrange("p (k n) -> p k n", n=HN),
                      d_cproj.rearrange("(k p) n -> p k n", p=128))
        m_allld = ("mld", sems["mld"][1], "gpsimd")
        m_dtp = m_ph = m_al = m_bst = m_cst = m_allld
        m_ci = m_o1 = m_oc = m_oh16 = m_b1 = m_b2 = m_rt = m_allld
        preload = [m_allld]

        waitm("scalar", m_dtp)
        mm = mark("scalar", nc.scalar.activation(out=p_ndt, in_=p_dt[:], func=AF.Abs), "ms")
        waitm("scalar", mm, force=True)
        mm = mark("scalar", nc.scalar.activation(out=p_ndt, in_=p_ndt, func=AF.Exp, scale=-1.0), "ms")
        waitm("scalar", mm, force=True)
        waitm("vector", mm)
        mm = mark("vector", nc.vector.tensor_scalar(out=p_ndt, in0=p_ndt[:], scalar1=1.0,
                                                    scalar2=None, op0=ALU.add), "mv")
        waitm("scalar", mm)
        m3 = mark("scalar", nc.scalar.activation(out=p_ndt, in_=p_ndt, func=AF.Ln), "ms")
        m4 = mark("scalar", nc.scalar.activation(out=p_decay[:, 0:KH], in_=p_dt[:], func=AF.Relu), "ms")
        waitm("vector", m3, m4)
        m_dt = mark("vector", nc.vector.tensor_add(out=p_dt[:], in0=p_decay[:, 0:KH], in1=p_ndt), "mv")
        waitm("vector", m_dt, force=True)
        m_ndt = mark("vector", nc.vector.tensor_scalar(out=p_ndt, in0=p_dt[:], scalar1=-1.0,
                                                       scalar2=None, op0=ALU.mult), "mv")
        waitm("scalar", m_ph)
        m_ph2 = mark("scalar", nc.scalar.activation(out=p_phabs[:], in_=p_phabs[:], func=AF.Abs), "ms")
        waitm("scalar", m_al)
        m_er = mark("scalar", nc.scalar.activation(out=r_erow, in_=r_erow, func=AF.Exp), "ms")
        b6 = pp7.acquire()
        waitm("tensor", m_er, m_o1)
        m_ebc = mark("tensor", nc.tensor.matmul(ps[b6][0:128, 0:HN], c_ones1[:], r_erow[:],
                                                start=True, stop=True), "mt")
        waitm("vector", m_ebc)
        m_ebc2 = mark("vector", nc.vector.tensor_copy(out=r_ebc[:, 0:HN], in_=ps[b6][0:128, 0:HN]), "mv")
        pp7.release(b6, m_ebc2)
        waitm("scalar", m_ebc2, m_ndt)
        last = None
        for k in range(KH):
            last = nc.scalar.activation(out=p_decay[:, k * HN:(k + 1) * HN], in_=r_ebc[:, 0:HN],
                                        func=AF.Exp, scale=p_ndt[:, k:k + 1])
        m_decay = mark("scalar", last, "ms")
        m_carry0 = mark("vector", nc.vector.memset(p_carry[:], 0.0), "mv")
        if debug:
            mark("vector", nc.vector.memset(s_ln2[0][:], 0.0), "mv")
            mark("vector", nc.vector.memset(s_ln2[1][:], 0.0), "mv")
            mark("vector", nc.vector.memset(s_h[:], 0.0), "mv")
            mark("vector", nc.vector.memset(s_y[:], 0.0), "mv")
            mark("vector", nc.vector.memset(x_att[:], 0.0), "mv")
            mark("vector", nc.vector.memset(x_mid[0][:], 0.0), "mv")
            mark("vector", nc.vector.memset(x_mid[1][:], 0.0), "mv")
        m_wrow0 = mark("vector", nc.vector.memset(r_wrow2[0][:], 0.0), "mv")
        m_wrow0b = mark("vector", nc.vector.memset(r_wrow2[1][:], 0.0), "mv")

        last_conv = None

        conv_totals = [0, 0]

        def conv(src_ap, dst_ap, vmk=None):
            s, m_l = pl_g.fetch(src_ap, vmk)
            waitm("sync", m_l)
            buf_src = pool_g[s][:] if vmk is None else vmk(pool_g[s])
            ins = nc.sync.dma_start(out=dst_ap, in_=buf_src)
            m = mark("sync", ins, f"mc{s}", 16)
            pl_g.consumed(s, m)

        def conv_multi(src_ap, vmk, stores):
            s, m_l = pl_g.fetch(src_ap, vmk)
            waitm("sync", m_l)
            for dst_ap, sview in stores:
                ins = nc.sync.dma_start(out=dst_ap, in_=sview(pool_g[s]))
                m = mark("sync", ins, f"mc{s}", 16)
            pl_g.consumed(s, m)

        conv_gate = {}

        def conv_w(e):
            for k in range(KD):
                for h in range(2):
                    conv_multi(
                        d_w1[e, k * 128:(k + 1) * 128, h * 2048:(h + 1) * 2048]
                        .rearrange("p (mgl msub c) -> p mgl msub c", msub=2, c=128),
                        lambda b: b[:].rearrange("p (mgl msub c) -> p mgl msub c", msub=2, c=128),
                        [(s_w1s[e, 8 * h:8 * h + 8, :, msub_ * 1024 + k * 128:
                                msub_ * 1024 + (k + 1) * 128].rearrange("mgl p c -> p mgl c"),
                          (lambda msub_=msub_: (lambda b: b[:].rearrange(
                              "p (mgl msub c) -> p mgl msub c", msub=2, c=128)[:, :, msub_, :]))())
                         for msub_ in range(2)])
            for j in range(16):
                kg_, ks0_ = (2 * j) // 16, (2 * j) % 16
                conv_multi(
                    d_w2[e, j * 256:(j + 1) * 256, :]
                    .rearrange("(k2s p) (f c) -> p k2s f c", p=128, c=128),
                    lambda b: b[:].rearrange("p (k2s f c) -> p k2s f c", f=8, c=128),
                    [(s_w2s[e, f_ * 2 + kg_, :, ks0_ * 128:(ks0_ + 2) * 128]
                      .rearrange("p (k2s c) -> p k2s c", c=128),
                      (lambda f_=f_: (lambda b: b[:].rearrange(
                          "p (k2s f c) -> p k2s f c", f=8, c=128)[:, :, f_, :]))())
                     for f_ in range(8)])
            conv_gate[("w", e)] = [("mc0", sems["mc0"][1], "sync"),
                                   ("mc1", sems["mc1"][1], "sync")]

        def conv_inps():
            for k in range(KD):
                for h in range(2):
                    conv_multi(
                        d_inp[k * 128:(k + 1) * 128, h * 2048:(h + 1) * 2048]
                        .rearrange("p (mgl msub c) -> p mgl msub c", msub=2, c=128),
                        lambda b: b[:].rearrange("p (mgl msub c) -> p mgl msub c", msub=2, c=128),
                        [(s_inps[8 * h:8 * h + 8, :, msub_ * 1024 + k * 128:
                                 msub_ * 1024 + (k + 1) * 128].rearrange("mgl p c -> p mgl c"),
                          (lambda msub_=msub_: (lambda b: b[:].rearrange(
                              "p (mgl msub c) -> p mgl msub c", msub=2, c=128)[:, :, msub_, :]))())
                         for msub_ in range(2)])
            conv_gate["inps"] = [("mc0", sems["mc0"][1], "sync"),
                                 ("mc1", sems["mc1"][1], "sync")]

        def conv_late(part):
            if part == 0:
                for j in range(16):
                    kg_, ks0_ = (2 * j) // 16, (2 * j) % 16
                    conv_multi(
                        d_outp[j * 256:(j + 1) * 256, :]
                        .rearrange("(ks p) (f c) -> p ks f c", p=128, c=128),
                        lambda b: b[:].rearrange("p (ks f c) -> p ks f c", f=8, c=128),
                        [(s_outps[f_ * 2 + kg_, :, ks0_ * 128:(ks0_ + 2) * 128]
                          .rearrange("p (ks c) -> p ks c", c=128),
                          (lambda f_=f_: (lambda b: b[:].rearrange(
                              "p (ks f c) -> p ks f c", f=8, c=128)[:, :, f_, :]))())
                         for f_ in range(8)])
            else:
                for k in range(KH // 2 * part - KH // 2, KH // 2 * part):
                    for jh in range(2):
                        conv(d_skip[k * 128:(k + 1) * 128, jh * 2048:(jh + 1) * 2048]
                             .rearrange("p (jl c) -> p jl c", c=128),
                             s_skps[16 * jh:16 * jh + 16, k // 16, :,
                                    (k % 16) * 128:(k % 16 + 1) * 128].rearrange("jl p c -> p jl c"),
                             lambda b: b[:].rearrange("p (jl c) -> p jl c", c=128))
                if part == 2:
                    conv_gate["late"] = [("mc0", sems["mc0"][1], "sync"),
                                         ("mc1", sems["mc1"][1], "sync")]

        def emit_ln(x_src, out_writer, xgate, rgate):
            sq_ms = []
            waitm("vector", *xgate, force=True)
            for k in range(KD):
                ii = nc.vector.tensor_mul(out=scratch8[:, k * T:(k + 1) * T],
                                          in0=x_src[:, k * T:(k + 1) * T],
                                          in1=x_src[:, k * T:(k + 1) * T])
                sq_ms.append(mark("vector", ii, "mv"))
            b6 = pp7.acquire()
            waitm("tensor", *xgate)
            for k in range(KD):
                ii = nc.tensor.matmul(ps[b6][0:1, 0:T], c_onescol[:],
                                      x_src[:, k * T:(k + 1) * T],
                                      start=(k == 0), stop=(k == KD - 1))
            m_sum = mark("tensor", ii, "mt")
            waitm("tensor", *sq_ms)
            for k in range(KD):
                ii = nc.tensor.matmul(ps[b6][0:1, 256:256 + T], c_onescol[:],
                                      scratch8[:, k * T:(k + 1) * T],
                                      start=(k == 0), stop=(k == KD - 1))
            m_sq = mark("tensor", ii, "mt")
            waitm("vector", m_sum, m_sq, rgate)
            ii = nc.vector.tensor_scalar(out=r_stat[0:1, 512:512 + T], in0=ps[b6][0:1, 0:T],
                                         scalar1=1.0 / D, scalar2=None, op0=ALU.mult)
            m1_ = mark("vector", ii, "mv")
            waitm("vector", m_sq, m1_, force=True)
            ii = nc.vector.tensor_scalar(out=r_stat[0:1, 0:T], in0=ps[b6][0:1, 256:256 + T],
                                         scalar1=1.0 / D, scalar2=None, op0=ALU.mult)
            m2_ = mark("vector", ii, "mv")
            pp7.release(b6, m2_)
            waitm("vector", m2_, force=True)
            ii = nc.vector.tensor_mul(out=r_stat[0:1, 256:256 + T],
                                      in0=r_stat[0:1, 512:512 + T],
                                      in1=r_stat[0:1, 512:512 + T])
            m3_ = mark("vector", ii, "mv")
            waitm("vector", m3_, force=True)
            ii = nc.vector.tensor_sub(out=r_stat[0:1, 0:T],
                                      in0=r_stat[0:1, 0:T],
                                      in1=r_stat[0:1, 256:256 + T])
            m4_ = mark("vector", ii, "mv")
            waitm("vector", m4_, force=True)
            ii = nc.vector.tensor_scalar(out=r_stat[0:1, 0:T], in0=r_stat[0:1, 0:T],
                                         scalar1=1e-5, scalar2=None, op0=ALU.add)
            m4_ = mark("vector", ii, "mv")
            waitm("scalar", m4_, rgate)
            ii = nc.scalar.activation(out=r_stat[0:1, 0:T], in_=r_stat[0:1, 0:T],
                                      func=AF.Ln)
            m5_ = mark("scalar", ii, "ms")
            waitm("scalar", m5_, force=True)
            ii = nc.scalar.activation(out=r_stat[0:1, 0:T], in_=r_stat[0:1, 0:T],
                                      func=AF.Exp, scale=-0.5)
            m6_ = mark("scalar", ii, "ms")
            waitm("vector", m6_)
            ii = nc.vector.scalar_tensor_tensor(out=r_stat[0:1, 256:256 + T],
                                                in0=r_stat[0:1, 512:512 + T], scalar=-1.0,
                                                in1=r_stat[0:1, 0:T],
                                                op0=ALU.mult, op1=ALU.mult)
            m7_ = mark("vector", ii, "mv")
            bC = ppC.acquire()
            waitm("tensor", m6_, m7_)
            ii = nc.tensor.matmul(ps[bC][:, 0:512], c_ones1[:], r_stat[0:1, 0:512],
                                  start=True, stop=True)
            m_ab = mark("tensor", ii, "mt")
            waitm("vector", m_ab, m_sq)
            app_ms = []
            for k in range(KD):
                app_ms.append(out_writer(k, ps[bC][:, 0:T], ps[bC][:, 256:256 + T]))
            ppC.release(bC, app_ms[-1])
            return app_ms, m_ab, m_sq

        def ln_apply(x_src, dst_ap_fn):
            def w(k, a_bc, b_bc):
                ii = nc.vector.tensor_mul(out=scratch8[:, k * T:(k + 1) * T],
                                          in0=x_src[:, k * T:(k + 1) * T], in1=a_bc)
                mm_ = mark("vector", ii, "mv")
                waitm("vector", mm_, force=True)
                ii = nc.vector.tensor_add(out=dst_ap_fn(k),
                                          in0=scratch8[:, k * T:(k + 1) * T], in1=b_bc)
                return mark("vector", ii, "mv")
            return w

        def emit_router(ln_cols_fn, wdst_half, ln_ms, wrow_dst):
            nonlocal last_qlg_reader
            b7 = pp7.acquire()
            waitm("tensor", *ln_ms)
            for k in range(KD):
                ii = nc.tensor.matmul(ps[b7][0:8, 0:T], p_router[:, k * NE:(k + 1) * NE],
                                      ln_cols_fn(k), start=(k == 0), stop=(k == KD - 1))
            m_lg = mark("tensor", ii, "mt")
            waitm("vector", m_lg)
            ii = nc.vector.tensor_copy(out=q_lg[:], in_=ps[b7][0:8, 0:T])
            m_lgs = mark("vector", ii, "mv")
            pp7.release(b7, m_lgs)
            b7 = pp7.acquire()
            waitm("tensor", m_lgs)
            t_ms = []
            for tt in range(2):
                ii = nc.tensor.transpose(ps[b7][0:128, tt * 8:tt * 8 + 8],
                                         q_lg[0:8, tt * 128:(tt + 1) * 128],
                                         c_ident[0:8, 0:8])
                t_ms.append(mark("tensor", ii, "mt"))
            last_qlg_reader = t_ms[-1]
            waitm("vector", *t_ms)
            ii = nc.vector.tensor_copy(out=q_tlg[:], in_=ps[b7][0:128, 0:16])
            m_tl = mark("vector", ii, "mv")
            pp7.release(b7, m_tl)
            waitm("vector", m_tl, force=True)

            def vv(ii):
                m = mark("vector", ii, "mv")
                waitm("vector", m, force=True)
                return m

            def g3(ap16):
                return ap16.rearrange("p (tt e) -> p tt e", e=8)

            def bc2(col):
                return q_m[:, col:col + 2].unsqueeze(2).broadcast_to([128, 2, 8])

            vv(nc.vector.tensor_reduce(out=q_m[:, 0:2], in_=g3(q_tlg[:]), axis=AX.X, op=ALU.max))
            vv(nc.vector.tensor_tensor(out=g3(q_t[:]), in0=g3(q_tlg[:]), in1=bc2(0),
                                       op=ALU.subtract))
            m_sb = (sems["mv"][0], None, None)
            m_sb = ("mv", sems["mv"][1], "vector")
            waitm("scalar", m_sb)
            ii = nc.scalar.activation(out=q_exps[:], in_=q_t[:], func=AF.Exp)
            m_ex = mark("scalar", ii, "ms")
            waitm("vector", m_ex)
            vv(nc.vector.tensor_reduce(out=q_m[:, 2:4], in_=g3(q_exps[:]), axis=AX.X, op=ALU.add))
            vv(nc.vector.reciprocal(out=q_m[:, 4:6], in_=q_m[:, 2:4]))
            vv(nc.vector.tensor_reduce(out=q_m[:, 0:2], in_=g3(q_exps[:]), axis=AX.X, op=ALU.max))
            vv(nc.vector.tensor_tensor(out=g3(q_t[:]), in0=g3(q_exps[:]), in1=bc2(0), op=ALU.is_lt))
            vv(nc.vector.tensor_mul(out=q_t[:], in0=q_exps[:], in1=q_t[:]))
            vv(nc.vector.tensor_reduce(out=q_m[:, 6:8], in_=g3(q_t[:]), axis=AX.X, op=ALU.max))
            vv(nc.vector.tensor_tensor(out=g3(q_t[:]), in0=g3(q_exps[:]), in1=bc2(6), op=ALU.is_ge))
            vv(nc.vector.tensor_mul(out=q_t[:], in0=q_exps[:], in1=q_t[:]))
            vv(nc.vector.tensor_tensor(out=g3(q_t[:]), in0=g3(q_t[:]), in1=bc2(4), op=ALU.mult))
            m_w = vv(nc.vector.tensor_scalar(out=q_wtm[:], in0=q_t[:], scalar1=0.5,
                                             scalar2=None, op0=ALU.mult))
            b7 = pp7.acquire()
            waitm("tensor", m_w)
            t_ms = []
            for tt in range(2):
                ii = nc.tensor.transpose(ps[b7][0:8, 16 + tt * 128:16 + (tt + 1) * 128],
                                         q_wtm[0:128, tt * 8:(tt + 1) * 8], c_ident[:])
                t_ms.append(mark("tensor", ii, "mt"))
            waitm("vector", *t_ms)
            ii = nc.vector.tensor_copy(out=wrow_dst[0:8, wdst_half * T:(wdst_half + 1) * T],
                                       in_=ps[b7][0:8, 16:16 + 256])
            m_wr = mark("vector", ii, "mv")
            pp7.release(b7, m_wr)
            return m_wr

        last_scan_readers = None
        hd_evac_done = None
        last_hd_reader = None
        last_hde_reader = None
        last_bbc_user = None
        last_lnj_reader = None
        last_sl2_reader = None
        last_sh_reader = None
        last_wrow_reader = None
        last_bcrows_reader = None
        last_rstat_reader = None
        last_xtm_reader = None
        last_xatt_readers = None
        last_store = None
        last_posbc = None
        last_qlg_reader = None
        wbc_hist = [None, None]
        xmid_statsread = [None, None]
        xmid_attnread = [None, None]
        xmid_written = [None, None]
        scan_done = None
        ffn2_evac_done = None
        pending_scan = []

        ln2_cols = lambda k: s_h[:, k * JT: k * JT + T]

        def queue_scan_dtile(dd, mi, pending, scan_ms):
            state = {"m_sc": None}

            def scan4(n0):
                def f():
                    waitm("vector", mi["bc_ready"], mi["hde_ready"], mi["y_free"])
                    for n in range(n0, n0 + 4):
                        col = dd * HN + n
                        waitm("vector", state["m_sc"], force=True)
                        ii = nc.vector.scalar_tensor_tensor(
                            out=s_v[0][:], in0=s_hde[:, dd * T:(dd + 1) * T],
                            scalar=p_dt[:, dd:dd + 1], in1=s_bbc[:, n * T:(n + 1) * T],
                            op0=ALU.mult, op1=ALU.mult)
                        m_v_ = mark("vector", ii, "mv")
                        waitm("vector", m_v_, force=True)
                        ii = nc.vector.tensor_tensor_scan(
                            out=s_hsc[:, n * T:(n + 1) * T],
                            data0=p_decay[:, col:col + 1].broadcast_to([128, T]),
                            data1=s_v[0][:],
                            initial=p_carry[:, col:col + 1],
                            op0=ALU.mult, op1=ALU.add)
                        state["m_sc"] = mark("vector", ii, "mv")
                return f

            def carry_mult():
                waitm("vector", state["m_sc"], force=True)
                ii = nc.vector.tensor_copy(
                    out=p_carry[:, dd * HN:(dd + 1) * HN],
                    in_=s_hsc[:].rearrange("p (n t) -> p n t", n=HN)[:, :, T - 1])
                m_c = mark("vector", ii, "mv")
                waitm("vector", m_c, force=True)
                ii = nc.vector.tensor_mul(out=s_hsc[:], in0=s_hsc[:], in1=s_cbc[:])
                m_m = mark("vector", ii, "mv")
                waitm("vector", m_m, force=True)
                ii = nc.vector.tensor_add(out=s_hsc[:, 0:8 * T], in0=s_hsc[:, 0:8 * T],
                                          in1=s_hsc[:, 8 * T:16 * T])
                m_m = mark("vector", ii, "mv")
                waitm("vector", m_m, force=True)

            def tree_y():
                w_ = 8 * T
                while w_ > 2 * T:
                    w_ //= 2
                    ii = nc.vector.tensor_add(out=s_hsc[:, 0:w_], in0=s_hsc[:, 0:w_],
                                              in1=s_hsc[:, w_:2 * w_])
                    m_m = mark("vector", ii, "mv")
                    waitm("vector", m_m, force=True)
                ii = nc.vector.tensor_add(out=s_y[:, dd * T:(dd + 1) * T],
                                          in0=s_hsc[:, 0:T], in1=s_hsc[:, T:2 * T])
                scan_ms.append(mark("vector", ii, "mv"))

            pending.append(scan4(0))
            pending.append(scan4(4))
            pending.append(scan4(8))
            pending.append(scan4(12))
            pending.append(carry_mult)
            pending.append(tree_y)

        ln1_ms_par = [[], []]
        ln3_ms_par = [[], []]
        wr1_par = [None, None]
        wr2_par = [None, None]
        lnj_reader_par = [None, None]
        wrow_reader_par = [None, None]

        def emit_xli(ch, par):
            nonlocal last_xtm_reader, last_rstat_reader
            waitm("gpsimd", last_xtm_reader)
            ins = nc.gpsimd.dma_start(
                out=x_tm[:].rearrange("p (tt c) -> p tt c", c=1024),
                in_=dx[ch * T:(ch + 1) * T, :].rearrange("(tt p) c -> p tt c", p=128))
            m_xl = mark("gpsimd", ins, "mx", 16)
            tr_ms = []
            for rnd in range(4):
                bC = ppC.acquire()
                waitm("tensor", m_xl)
                ev_src = []
                for q in range(4):
                    idx = rnd * 4 + q
                    tt, k = idx // KD, idx % KD
                    ii = nc.tensor.transpose(
                        ps[bC][:, q * 128:(q + 1) * 128],
                        x_tm[:, tt * 1024 + k * 128:tt * 1024 + (k + 1) * 128],
                        c_ident[:])
                    ev_src.append((q, tt, k, mark("tensor", ii, "mt")))
                for q, tt, k, m_ in ev_src:
                    waitm("vector", ev_src[-1][3])
                    ii = nc.vector.tensor_copy(
                        out=x_fin[:, k * T + tt * 128:k * T + (tt + 1) * 128],
                        in_=ps[bC][:, q * 128:(q + 1) * 128])
                    m_e = mark("vector", ii, "mv")
                ppC.release(bC, m_e)
                tr_ms.append(m_e)
            last_xtm_reader = ev_src[-1][3]
            waitm("vector", last_store, lnj_reader_par[par])
            w = ln_apply(x_fin, lambda k: s_ln2[par][:, k * JT:k * JT + T])
            ms, m_ab, _ = emit_ln(x_fin, w, tr_ms, last_rstat_reader)
            last_rstat_reader = m_ab
            ln1_ms_par[par] = ms
            waitm("vector", wrow_reader_par[par])
            wr1_par[par] = emit_router(
                lambda k: s_ln2[par][:, k * JT:k * JT + T], 0, ms, r_wrow2[par])

        def emit_late(it, scan_ms, drain_scan):
            nonlocal last_hd_reader, last_scan_readers, last_rstat_reader
            par = (it + 1) % 2
            p_rd = it % 2
            skip_ev = []
            m_g_ = None
            for j in range(KH):
                while len(scan_ms) <= j and pending_scan:
                    drain_scan(1)
                fetch_ms, slots = [], []
                for kg in range(2):
                    waitm("gpsimd", *conv_gate["late"])
                    waitm("sync", *conv_gate["late"], force=True)
                    pl_, s_, m_ = fetch2(kg, s_skps[j, kg, :, :])
                    slots.append((pl_, s_))
                    fetch_ms.append(m_)
                bB = ppB.acquire()
                waitm("tensor", hd_evac_done, scan_ms[j])
                for k in range(KH):
                    waitm("tensor", fetch_ms[k // 16])
                    ii = nc.tensor.matmul(
                        ps[bB][:, 0:T],
                        slots[k // 16][0].bufs[slots[k // 16][1]][:, (k % 16) * 128:(k % 16 + 1) * 128],
                        s_hd[:, k * T:(k + 1) * T],
                        start=(k == 0), stop=(k == KH - 1))
                m_g_ = mark("tensor", ii, "mt")
                for pl_, s_ in slots:
                    pl_.consumed(s_, m_g_)
                waitm("vector", m_g_)
                ii = nc.vector.tensor_add(out=s_y[:, j * T:(j + 1) * T],
                                          in0=s_y[:, j * T:(j + 1) * T],
                                          in1=ps[bB][:, 0:T])
                m_e = mark("vector", ii, "mv")
                ppB.release(bB, m_e)
                skip_ev.append(m_e)
            last_hd_reader = m_g_
            attn_ev = []
            for f in range(KD):
                fetch_ms, slots = [], []
                for kg in range(2):
                    waitm("gpsimd", *conv_gate["late"])
                    waitm("sync", *conv_gate["late"], force=True)
                    pl_, s_, m_ = fetch2(kg, s_outps[f * 2 + kg, :, :])
                    slots.append((pl_, s_))
                    fetch_ms.append(m_)
                bB = ppB.acquire()
                waitm("tensor", *skip_ev)
                for k in range(KH):
                    waitm("tensor", fetch_ms[k // 16])
                    ii = nc.tensor.matmul(
                        ps[bB][:, 0:T],
                        slots[k // 16][0].bufs[slots[k // 16][1]][:, (k % 16) * 128:(k % 16 + 1) * 128],
                        s_y[:, k * T:(k + 1) * T],
                        start=(k == 0), stop=(k == KH - 1))
                m_g_ = mark("tensor", ii, "mt")
                for pl_, s_ in slots:
                    pl_.consumed(s_, m_g_)
                waitm("vector", m_g_, last_xatt_readers)
                ii = nc.vector.tensor_add(out=x_att[:, f * T:(f + 1) * T],
                                          in0=x_mid[p_rd][:, f * T:(f + 1) * T],
                                          in1=ps[bB][:, 0:T])
                m_e = mark("vector", ii, "mv")
                ppB.release(bB, m_e)
                attn_ev.append(m_e)
            last_scan_readers = m_g_
            xmid_attnread[p_rd] = attn_ev[-1]
            waitm("vector", last_store, lnj_reader_par[par])
            w = ln_apply(x_att, lambda k: s_ln2[par][:, k * JT + T:(k + 1) * JT])
            ms, m_ab, _ = emit_ln(x_att, w, attn_ev, last_rstat_reader)
            last_rstat_reader = m_ab
            ln3_ms_par[par] = ms
            waitm("vector", wrow_reader_par[par])
            wr2_par[par] = emit_router(
                lambda k: s_ln2[par][:, k * JT + T:(k + 1) * JT], 1, ms, r_wrow2[par])

        waitm("gpsimd", m_decay)
        emit_xli(0, 1)

        for it in range(-1, NCH + 1):
            scan_ms_cur = []
            par = it % 2
            has_pre = 0 <= it <= NCH - 1
            has_ffn1 = -1 <= it <= NCH - 2
            has_ffn2 = 1 <= it <= NCH
            has_store = 1 <= it <= NCH
            p_wr = (it + 1) % 2
            p_rd = it % 2

            if has_pre:
                waitm("vector", last_sh_reader, last_store)
                w = ln_apply(x_mid[p_rd], ln2_cols)
                ln2_ms, m_ab2, m_stats2 = emit_ln(
                    x_mid[p_rd], w,
                    [xmid_written[p_rd], xmid_attnread[p_rd]], last_rstat_reader)
                last_rstat_reader = m_ab2
                xmid_statsread[p_rd] = m_stats2
                waitm("gpsimd", last_posbc, last_qlg_reader)
                ins = nc.gpsimd.dma_start(out=q_lg[0:1, :], in_=d_pos[0:1, it * T:(it + 1) * T])
                m_pl = mark("gpsimd", ins, "mp", 16)
                bCpos = ppC.acquire()
                waitm("tensor", m_pl)
                ii = nc.tensor.matmul(ps[bCpos][:, 0:T], c_ones1[:], q_lg[0:1, :],
                                      start=True, stop=True)
                last_posbc = mark("tensor", ii, "mt")
                pos_bc_ap = ps[bCpos][:, 0:T]
                m_pb = last_posbc
                hd_ev, hde_ms = [], []
                sin_pend = [None, None]
                for mg in range(16):
                    waitm("gpsimd", *conv_gate["inps"])
                    waitm("sync", *conv_gate["inps"], force=True)
                    pl_, s_, m_f = fetch2(mg, s_inps[mg, :, :])
                    for msub in range(2):
                        m = mg * 2 + msub
                        bA = ppA.acquire()
                        waitm("tensor", m_f, *ln2_ms)
                        for k in range(KD):
                            ii = nc.tensor.matmul(
                                ps[bA][:, 0:T],
                                pl_.bufs[s_][:, (msub * KD + k) * 128:(msub * KD + k + 1) * 128],
                                ln2_cols(k), start=(k == 0), stop=(k == KD - 1))
                        m_g_ = mark("tensor", ii, "mt")
                        if msub == 1:
                            pl_.consumed(s_, m_g_)
                        waitm("scalar", m_g_, last_hd_reader)
                        ii = nc.scalar.activation(out=s_hd[:, m * T:(m + 1) * T],
                                                  in_=ps[bA][:, 0:T], func=AF.Copy)
                        m_hd = mark("scalar", ii, "ms")
                        ppA.release(bA, m_hd)
                        hd_ev.append(m_hd)
                        st0 = (m % 2) * 768
                        c0 = scratch8[:, st0:st0 + T]
                        c1 = scratch8[:, st0 + 256:st0 + 256 + T]
                        cs = scratch8[:, st0 + 512:st0 + 512 + T]
                        waitm("vector", m_pb, sin_pend[m % 2], last_store)
                        ii = nc.vector.tensor_scalar(out=c0, in0=pos_bc_ap,
                                                     scalar1=p_phabs[:, m:m + 1],
                                                     scalar2=HALFPI, op0=ALU.mult, op1=ALU.add)
                        mm_ = mark("vector", ii, "mv")
                        if m == KH - 1:
                            ppC.release(bCpos, mm_)
                        waitm("vector", mm_, force=True)
                        ii = nc.vector.tensor_scalar(out=c1, in0=c0, scalar1=INV2PI,
                                                     scalar2=None, op0=ALU.mult)
                        mm_ = mark("vector", ii, "mv")
                        waitm("vector", mm_, force=True)
                        ii = nc.vector.tensor_scalar(out=c1, in0=c1, scalar1=8388608.0,
                                                     scalar2=8388608.0, op0=ALU.add,
                                                     op1=ALU.subtract)
                        mm_ = mark("vector", ii, "mv")
                        waitm("vector", mm_, force=True)
                        ii = nc.vector.scalar_tensor_tensor(out=c0, in0=c1, scalar=-TWOPI,
                                                            in1=c0, op0=ALU.mult, op1=ALU.add)
                        mm_ = mark("vector", ii, "mv")
                        waitm("vector", mm_, force=True)
                        ii = nc.vector.tensor_scalar(out=c0, in0=c0, scalar1=-3.1415925,
                                                     scalar2=3.1415925, op0=ALU.max,
                                                     op1=ALU.min)
                        m_r = mark("vector", ii, "mv")
                        waitm("scalar", m_r)
                        ii = nc.scalar.activation(out=cs, in_=c0, func=AF.Sin)
                        m_sin = mark("scalar", ii, "ms")
                        sin_pend[m % 2] = m_sin
                        waitm("vector", m_sin, m_hd, last_hde_reader)
                        ii = nc.vector.tensor_mul(out=s_hde[:, m * T:(m + 1) * T],
                                                  in0=s_hd[:, m * T:(m + 1) * T], in1=cs)
                        hde_ms.append(mark("vector", ii, "mv"))
                hd_evac_done = hd_ev[-1]
                last_sl2_reader = m_g_
                b6 = pp7.acquire()
                for k in range(KH):
                    waitm("tensor", hde_ms[k])
                    ii = nc.tensor.matmul(ps[b6][0:16, 0:T], p_bstat[:, k * HN:(k + 1) * HN],
                                          s_hde[:, k * T:(k + 1) * T],
                                          start=(k == 0), stop=(k == KH - 1))
                m_bp = mark("tensor", ii, "mt")
                for k in range(KH):
                    ii = nc.tensor.matmul(ps[b6][0:16, 256:256 + T], p_cstat[:, k * HN:(k + 1) * HN],
                                          s_hde[:, k * T:(k + 1) * T],
                                          start=(k == 0), stop=(k == KH - 1))
                m_cp = mark("tensor", ii, "mt")
                last_hde_reader = m_cp
                waitm("vector", m_bp, m_cp)
                m_bcr = mark("vector", nc.vector.tensor_copy(out=r_bc[:], in_=ps[b6][0:16, 0:T]), "mv")
                waitm("vector", m_bcr, force=True)
                m_ccr = mark("vector", nc.vector.tensor_copy(out=r_cc[:], in_=ps[b6][0:16, 256:256 + T]), "mv")
                pp7.release(b6, m_ccr)
                bc_ev = []
                for n in range(HN):
                    bC = ppC.acquire()
                    waitm("tensor", m_bcr, last_bcrows_reader)
                    ii = nc.tensor.matmul(ps[bC][:, 0:T],
                                          c_oh16[0:16, n * 128:(n + 1) * 128], r_bc[:],
                                          start=True, stop=True)
                    m_ = mark("tensor", ii, "mt")
                    waitm("scalar", m_, last_bbc_user)
                    ii = nc.scalar.activation(out=s_bbc[:, n * T:(n + 1) * T],
                                              in_=ps[bC][:, 0:T], func=AF.Copy)
                    m_e = mark("scalar", ii, "ms")
                    ppC.release(bC, m_e)
                    bc_ev.append(m_e)
                for n in range(HN):
                    bC = ppC.acquire()
                    waitm("tensor", m_ccr)
                    ii = nc.tensor.matmul(ps[bC][:, 0:T],
                                          c_oh16[0:16, n * 128:(n + 1) * 128], r_cc[:],
                                          start=True, stop=True)
                    m_ = mark("tensor", ii, "mt")
                    last_bcrows_reader = m_
                    waitm("scalar", m_, last_bbc_user)
                    ii = nc.scalar.activation(
                        out=s_cbc[:, n * T:(n + 1) * T],
                        in_=ps[bC][:, 0:T], func=AF.Copy)
                    m_e = mark("scalar", ii, "ms")
                    ppC.release(bC, m_e)
                    bc_ev.append(m_e)
                gates = {"bc_ready": bc_ev[-1], "hde_ready": hde_ms[-1],
                         "y_free": last_scan_readers}
                for dd in range(KH):
                    queue_scan_dtile(dd, gates, pending_scan, scan_ms_cur)

            n_halves = (1 if has_ffn1 else 0) + (1 if has_ffn2 else 0)
            scan_ms = scan_ms_cur

            def drain_scan(count):
                for _ in range(count):
                    if not pending_scan:
                        return
                    pending_scan.pop(0)()

            if n_halves:
                s_ln = s_ln2[par]
                r_wrow = r_wrow2[par]
                halves = []
                if has_ffn1:
                    halves.append((0, x_mid[p_wr], x_fin))
                if has_ffn2:
                    halves.append((1, x_att, x_att))
                wgate = [wr1_par[par] if has_ffn1 else None,
                         wr2_par[par] if has_ffn2 else None]
                init_ms = []
                for f in range(KD):
                    bB = ppB.acquire()
                    waitm("tensor", *wgate)
                    ii = nc.tensor.matmul(ps[bB][:, 0:JT],
                                          p_b2[0:8, f * 128:(f + 1) * 128], r_wrow[:],
                                          start=True, stop=True)
                    m_ = mark("tensor", ii, "mt")
                    waitm("vector", m_)
                    for hf, dst, base in halves:
                        if hf == 0:
                            waitm("vector", xmid_statsread[p_wr], xmid_attnread[p_wr])
                        ii = nc.vector.tensor_add(
                            out=dst[:, f * T:(f + 1) * T],
                            in0=base[:, f * T:(f + 1) * T],
                            in1=ps[bB][:, hf * T:hf * T + T])
                        m_e = mark("vector", ii, "mv")
                    ppB.release(bB, m_e)
                    init_ms.append(m_e)
                hloop_last = None
                for e in range(NE):
                    if it == -1:
                        conv_w(e)
                    elif it == 0 and e == 0:
                        conv_late(0)
                    elif it == 0 and e in (2, 4):
                        conv_late(e // 2)
                    wsl = e % 2
                    bC = ppC.acquire()
                    waitm("tensor", *wgate)
                    ii = nc.tensor.matmul(ps[bC][:, 0:JT],
                                          c_oh16[0:8, e * 128:(e + 1) * 128], r_wrow[:],
                                          start=True, stop=True)
                    m_ = mark("tensor", ii, "mt")
                    wrow_reader_par[par] = m_
                    waitm("scalar", m_, wbc_hist[wsl])
                    ii = nc.scalar.activation(out=s_wbc[:, wsl * JT:(wsl + 1) * JT],
                                              in_=ps[bC][:, 0:JT], func=AF.Copy)
                    m_wbc = mark("scalar", ii, "ms")
                    ppC.release(bC, m_wbc)
                    waitm("sync", *conv_gate[("w", e)], force=True)
                    waitm("gpsimd", *conv_gate[("w", e)])
                    for mg in range(16):
                        s_, m_f = pl_s.fetch(s_w1s[e, mg, :, :])
                        for msub in range(2):
                            m = mg * 2 + msub
                            bA = ppA.acquire()
                            if has_ffn1:
                                waitm("tensor", *ln1_ms_par[par])
                            if has_ffn2:
                                waitm("tensor", *ln3_ms_par[par])
                            waitm("tensor", m_f)
                            for k in range(KD):
                                wtile = pool_s[s_][:, (msub * KD + k) * 128:(msub * KD + k + 1) * 128]
                                if n_halves == 2:
                                    ii = nc.tensor.matmul(ps[bA][:, 0:JT], wtile,
                                                          s_ln[:, k * JT:(k + 1) * JT],
                                                          start=(k == 0), stop=(k == KD - 1))
                                else:
                                    hf = halves[0][0]
                                    ii = nc.tensor.matmul(ps[bA][:, 0:T], wtile,
                                                          s_ln[:, k * JT + hf * T:k * JT + (hf + 1) * T],
                                                          start=(k == 0), stop=(k == KD - 1))
                            m_g_ = mark("tensor", ii, "mt")
                            hloop_last = m_g_
                            if msub == 1:
                                pl_s.consumed(s_, m_g_)
                            waitm("scalar", m_g_)
                            if e == 0:
                                waitm("scalar", last_sl2_reader)
                            width = JT if n_halves == 2 else T
                            ii = nc.scalar.activation(
                                out=s_h[:, m * JT:m * JT + width],
                                in_=ps[bA][:, 0:width], func=AF.Gelu,
                                bias=p_b1[:, e * KH + m:e * KH + m + 1])
                            m_ge = mark("scalar", ii, "ms")
                            ppA.release(bA, m_ge)
                        drain_scan(1)
                    for f in range(KD):
                        fetch_ms, slots = [], []
                        for kg in range(2):
                            pl_, s_, m_ = fetch2(kg, s_w2s[e, f * 2 + kg, :, :])
                            slots.append((pl_, s_))
                            fetch_ms.append(m_)
                        bB = ppB.acquire()
                        waitm("tensor", m_ge)
                        width = JT if n_halves == 2 else T
                        for k2 in range(KH):
                            waitm("tensor", fetch_ms[k2 // 16])
                            ii = nc.tensor.matmul(
                                ps[bB][:, 0:width],
                                slots[k2 // 16][0].bufs[slots[k2 // 16][1]][:, (k2 % 16) * 128:(k2 % 16 + 1) * 128],
                                s_h[:, k2 * JT:k2 * JT + width],
                                start=(k2 == 0), stop=(k2 == KH - 1))
                        m_g_ = mark("tensor", ii, "mt")
                        for pl_, s_ in slots:
                            pl_.consumed(s_, m_g_)
                        if e == NE - 1 and f == KD - 1:
                            last_sh_reader = m_g_
                        waitm("vector", m_g_, m_wbc)
                        qcol = (f % 2) * 512
                        for idx, (hf, dst, base) in enumerate(halves):
                            pcol = hf * T if n_halves == 2 else 0
                            wcol = wsl * JT + hf * T
                            ii = nc.vector.tensor_tensor(
                                out=scratch8[:, qcol + idx * T:qcol + (idx + 1) * T],
                                in0=ps[bB][:, pcol:pcol + T],
                                in1=s_wbc[:, wcol:wcol + T], op=ALU.mult)
                            m_q = mark("vector", ii, "mv")
                            waitm("vector", m_q, force=True)
                            ii = nc.vector.tensor_add(
                                out=dst[:, f * T:(f + 1) * T],
                                in0=dst[:, f * T:(f + 1) * T],
                                in1=scratch8[:, qcol + idx * T:qcol + (idx + 1) * T])
                            m_e = mark("vector", ii, "mv")
                        ppB.release(bB, m_e)
                        if f == KD - 1:
                            wbc_hist[wsl] = m_e
                        if e == NE - 1 and has_ffn2:
                            ffn2_evac_done = m_e
                        drain_scan(1)
                    if e == 1 and it + 2 <= NCH - 1:
                        emit_xli(it + 2, (it + 1) % 2)
                lnj_reader_par[par] = hloop_last
                if has_ffn1:
                    xmid_written[p_wr] = m_e
            if not has_pre:
                drain_scan(KH)

            if it == -1:
                conv_inps()

            if has_store:
                st_ev = []
                for rnd in range(4):
                    bC = ppC.acquire()
                    waitm("tensor", ffn2_evac_done)
                    ev_src = []
                    for q in range(4):
                        idx = rnd * 4 + q
                        tt, k = idx // KD, idx % KD
                        ii = nc.tensor.transpose(
                            ps[bC][:, q * 128:(q + 1) * 128],
                            x_att[:, k * T + tt * 128:k * T + (tt + 1) * 128],
                            c_ident[:])
                        ev_src.append((q, tt, k, mark("tensor", ii, "mt")))
                    for q, tt, k, m_ in ev_src:
                        waitm("vector", ev_src[-1][3], last_store)
                        ii = nc.vector.tensor_copy(
                            out=scratch8[:, tt * 1024 + k * 128:tt * 1024 + (k + 1) * 128],
                            in_=ps[bC][:, q * 128:(q + 1) * 128])
                        m_e = mark("vector", ii, "mv")
                        st_ev.append(m_e)
                    ppC.release(bC, m_e)
                last_xatt_readers = ev_src[-1][3]
                waitm("gpsimd", *st_ev)
                ch = it - 1
                ins = nc.gpsimd.dma_start(
                    out=d_out[ch * T:(ch + 1) * T, :].rearrange("(tt p) c -> p tt c", p=128),
                    in_=scratch8[:].rearrange("p (tt c) -> p tt c", c=1024))
                last_store = mark("gpsimd", ins, "mst", 16)

            if has_pre:
                emit_late(it, scan_ms, drain_scan)
                scan_done = scan_ms[-1]
                last_bbc_user = scan_ms[-1]

        nc.gpsimd.wait_ge(sems["mst"][0], sems["mst"][1])
        nc.vector.wait_ge(sems["mst"][0], sems["mst"][1])

    return nc


def _host_layouts(inputs):
    f32 = np.float32
    bf16 = ml_dtypes.bfloat16
    inp = {
        "inproj": np.ascontiguousarray(inputs["L0_attn_input_proj"], dtype=f32),
        "outproj": np.ascontiguousarray(inputs["L0_attn_output_proj"], dtype=f32),
        "alog": np.ascontiguousarray(np.asarray(inputs["L0_attn_a_log"], dtype=f32).reshape(1, HN)),
        "bproj": np.ascontiguousarray(inputs["L0_attn_b_proj"], dtype=f32),
        "cproj": np.ascontiguousarray(inputs["L0_attn_c_proj"], dtype=f32),
        "dtp": np.ascontiguousarray(np.asarray(inputs["L0_attn_dt_proj"], dtype=f32).reshape(KH, 128).T),
        "skip": np.ascontiguousarray(inputs["L0_attn_skip_proj"], dtype=f32),
        "phase": np.ascontiguousarray(np.asarray(inputs["L0_attn_floquet_phases"], dtype=f32).reshape(KH, 128).T),
        "routerT": np.ascontiguousarray(
            np.asarray(inputs["L0_ffn_w_router"], dtype=f32).reshape(KD, 128, NE)
            .transpose(1, 0, 2).reshape(128, KD * NE)),
        "w1": np.ascontiguousarray(inputs["L0_ffn_w1"], dtype=f32),
        "b1t": np.ascontiguousarray(
            np.asarray(inputs["L0_ffn_b1"], dtype=f32).reshape(NE, KH, 128)
            .transpose(2, 0, 1).reshape(128, NE * KH)),
        "w2": np.ascontiguousarray(inputs["L0_ffn_w2"], dtype=f32),
        "b2": np.ascontiguousarray(inputs["L0_ffn_b2"], dtype=f32),
        "ident": np.eye(128, dtype=f32),
        "ones1": np.ones((1, 128), f32),
        "onescol": np.ones((128, 1), f32),
    }
    oh16 = np.zeros((16, 16 * 128), f32)
    for n in range(16):
        oh16[n, n * 128:(n + 1) * 128] = 1.0
    inp["oh16"] = oh16.astype(bf16)
    oh8 = np.zeros((8, 8 * 128), f32)
    for e in range(8):
        oh8[e, e * 128:(e + 1) * 128] = 1.0
    inp["oh8"] = oh8.astype(bf16)
    return inp


def kernel(**inputs):
    x = np.asarray(inputs["L0_attn_x"], np.float32)
    B, L, _ = x.shape
    shared = _host_layouts(inputs)
    shared["posr"] = np.arange(L, dtype=np.float32).reshape(1, L)
    nc = build_program(L)
    in_maps = []
    for b in range(B):
        m = dict(shared)
        m["x"] = np.ascontiguousarray(x[b])
        in_maps.append(m)
    res = run_bass_kernel_spmd(nc, in_maps, core_ids=list(range(B)))
    out = np.stack([res.results[b]["out"] for b in range(B)], axis=0)
    return out.astype(np.float32)
